# version 01 (final)
# baseline (speedup 1.0000x reference)
import os
from contextlib import ExitStack
import numpy as np

import concourse.bass as bass
import concourse.mybir as mybir
from concourse.bass import ds
from concourse.bass_utils import run_bass_kernel_spmd

F32 = mybir.dt.float32
BF16 = mybir.dt.bfloat16
I32 = mybir.dt.int32
AF = mybir.ActivationFunctionType
ALU = mybir.AluOpType
ET = mybir.EngineType

D = 2048
KC = 16
T = 2048
F = 1024
S = 4096
NE = 16
EPS = 1e-5
QK_SCALE = float(128.0 ** -0.25)
MAGIC = np.array([1234567.0, -7654321.0, 33445566.0, -11223344.0], np.float32)
TCH = 512

N_PHASES = int(os.environ.get("BASS_MK_PHASES", "4"))
TAP = os.environ.get("BASS_MK_TAP", "")
NOBAR = bool(os.environ.get("BASS_MK_NOBAR"))
HARDEN = int(os.environ.get("BASS_MK_HARDEN", "1"))
N_HEADS = int(os.environ.get("BASS_MK_NHEADS", "16"))


class _StopEmission(Exception):
    pass



def _core_blocks(c):
    r = c % 2
    g = (c // 4) % 2
    blocks = [0, 7, 1, 6] if r == 0 else [2, 5, 3, 4]
    if g == 1:
        blocks = blocks[2:] + blocks[:2]
    return blocks


def _batch_of_core(c):
    return (c // 2) % 2


def _colblocks(w, nb):
    return np.ascontiguousarray(
        w.reshape(KC, 128, nb, 128).transpose(2, 1, 0, 3).reshape(nb, 128, KC * 128))


def _host_inputs(inputs):
    x = np.ascontiguousarray(inputs["L0_attn_a_X"])

    w_in = np.asarray(inputs["L0_attn_a_W_in"], np.float32)
    w_out_a = np.asarray(inputs["L0_attn_a_W_out"], np.float32)
    w_qkv = np.asarray(inputs["L0_attn_b_w_qkv"], np.float32)
    w_outb = np.asarray(inputs["L0_attn_b_w_out"], np.float32)
    we1 = np.asarray(inputs["L0_ffn_we1"], np.float32)
    we3 = np.asarray(inputs["L0_ffn_we3"], np.float32)
    we2 = np.asarray(inputs["L0_ffn_we2"], np.float32)
    ws1 = np.asarray(inputs["L0_ffn_ws1"], np.float32)
    ws3 = np.asarray(inputs["L0_ffn_ws3"], np.float32)
    ws2 = np.asarray(inputs["L0_ffn_ws2"], np.float32)
    gamma = np.asarray(inputs["L0_attn_a_gamma"], np.float32)
    beta = np.asarray(inputs["L0_attn_a_beta"], np.float32)

    def expert_blocks(w):
        return np.ascontiguousarray(
            w.reshape(NE, KC, 128, 8, 128).transpose(0, 3, 2, 1, 4)
            .reshape(NE, 8, 128, KC * 128))

    shared_w = dict(
        w_in_q=_colblocks(w_in[:, :D], KC),
        w_in_kv=np.ascontiguousarray(
            w_in[:, D:].reshape(KC, 128, 8, 512).transpose(2, 0, 1, 3)),
        w_out_a=_colblocks(w_out_a, KC),
        w_qkv_qk=_colblocks(np.ascontiguousarray(w_qkv.T[:, : 2 * D]), 2 * KC),
        w_qkv_v=np.ascontiguousarray(
            w_qkv.T[:, 2 * D:].reshape(KC, 128, 4, 512).transpose(2, 0, 1, 3)),
        w_outb=_colblocks(np.ascontiguousarray(w_outb.T), KC),
        w_router=np.ascontiguousarray(np.asarray(inputs["L0_ffn_w_router"], np.float32)),
        we1b=expert_blocks(we1),
        we3b=expert_blocks(we3),
        we2r=np.ascontiguousarray(we2.reshape(NE, 8, 128, D)),
        ws1b=_colblocks(ws1, 32),
        ws3b=_colblocks(ws3, 32),
        ws2r=np.ascontiguousarray(ws2.reshape(32, 128, D)),
        b_in=np.asarray(inputs["L0_attn_a_b_in"], np.float32).reshape(1, 3 * D),
        b_out_a=np.asarray(inputs["L0_attn_a_b_out"], np.float32).reshape(1, D),
        b_qkv=np.asarray(inputs["L0_attn_b_b_qkv"], np.float32).reshape(1, 3 * D),
        b_outb=np.asarray(inputs["L0_attn_b_b_out"], np.float32).reshape(1, D),
        gb_cols=np.ascontiguousarray(
            np.concatenate([gamma.reshape(KC, 128).T, beta.reshape(KC, 128).T], 1)),
        identity=np.eye(128, dtype=np.float32),
        magic=MAGIC.reshape(1, 4),
    )

    inv_freq = 1.0 / (10000.0 ** (np.arange(0, 1024, 2, dtype=np.float64) / 1024.0))
    pos = np.arange(S, dtype=np.float64)
    fr = pos[:, None] * inv_freq[None, :]
    cos_full = np.repeat(np.cos(fr), 2, axis=1).astype(np.float32)
    sin_signed = np.repeat(np.sin(fr), 2, axis=1).astype(np.float32)
    sin_signed[:, 0::2] *= -1.0

    in_maps = []
    for c in range(8):
        b = _batch_of_core(c)
        blocks = _core_blocks(c)
        spine_pos = np.concatenate(
            [np.arange(j * 512, (j + 1) * 512) for j in blocks])
        flesh_pos = spine_pos[:F]
        smine = sorted(blocks)
        wcol = np.array([[smine.index(blocks[tsl]) * 512 for tsl in range(4)]],
                        np.int32)
        qp = (flesh_pos[None, :].astype(np.float32)
              - np.arange(128, dtype=np.float32)[:, None])
        pen2 = np.full((128, 64), -1e30, np.float32)
        bmask = np.zeros((128, 8 * 512), np.float32)
        for qt in range(8):
            qp_ = flesh_pos[qt * 128:(qt + 1) * 128]
            for ch in range(8):
                pen2[:, qt * 8 + ch] = np.where(
                    512 * (ch + 1) - 1 <= qp_, 0.0, -1e30)
            cb = blocks[qt // 4]
            bmask[:, qt * 512:(qt + 1) * 512] = (
                (cb * 512 + np.arange(512))[None, :] <= qp_[:, None])
        penb = (bmask - 1.0) * 1e30
        im = dict(shared_w)
        im.update(
            x_sp=np.ascontiguousarray(x[b, spine_pos, :].T),
            cos_t=np.ascontiguousarray(cos_full[spine_pos]),
            sin_t=np.ascontiguousarray(sin_signed[spine_pos]),
            wcol_t=wcol,
            qp_t=np.ascontiguousarray(qp),
            pen2_t=pen2,
            bmask_t=bmask.astype(np.float32),
            penb_t=penb.astype(np.float32),
            parity=np.array([[c % 2]], np.int32),
        )
        in_maps.append(im)
    return in_maps


def _host_output(results):
    out_full = np.zeros((2, S, D), np.float32)
    for c in range(8):
        b = _batch_of_core(c)
        blocks = _core_blocks(c)[:2]
        o = np.asarray(results[c]["out"])
        out_full[b, blocks[0] * 512:(blocks[0] + 1) * 512, :] = o[:, 0:512].T
        out_full[b, blocks[1] * 512:(blocks[1] + 1) * 512, :] = o[:, 512:1024].T
    return out_full



_CHAIN_SKIP = {
    "dma_start", "dma_start_transpose", "wait_ge", "nop", "engine_nop",
    "value_load", "reg_load", "reg_mov", "reg_add", "reg_alu", "register",
    "While", "compute_val",
}


class Chained:

    def __init__(self, cx, eng, sname):
        self._cx = cx
        self._eng = eng
        self._sname = sname

    def __getattr__(self, name):
        real = getattr(self._eng, name)
        if name in _CHAIN_SKIP or name.startswith("_"):
            return real

        def wrapped(*a, **k):
            cx = self._cx
            self._eng.wait_ge(cx.h[self._sname], cx.cnt[self._sname])
            inst = real(*a, **k)
            inst.then_inc(cx.h[self._sname], 1)
            cx.cnt[self._sname] += 1
            return inst

        return wrapped


class Ctx:
    def __init__(self, nc):
        self.nc = nc
        self._cms = []
        self.h = {}
        self.cnt = {}
        self.E = dict(sync=nc.sync, scalar=nc.scalar, vector=nc.vector,
                      tensor=nc.tensor, gpsimd=nc.gpsimd)

    def sem(self, name):
        cm = self.nc.semaphore(name)
        self.h[name] = cm.__enter__()
        self.cnt[name] = 0
        self._cms.append(cm)

    def dma(self, eng, sname, dst, src, **kw):
        self.E[eng].dma_start(dst, src, **kw).then_inc(self.h[sname], 16)
        self.cnt[sname] += 16
        return self.cnt[sname]

    def tick(self, inst, sname, by=1):
        inst.then_inc(self.h[sname], by)
        self.cnt[sname] += by
        return self.cnt[sname]

    def wait(self, eng, sname, val=None):
        self.E[eng].wait_ge(self.h[sname], self.cnt[sname] if val is None else val)

    def close(self):
        for cm in reversed(self._cms):
            cm.__exit__(None, None, None)


def build_program():
    try:
        _build_program_inner()
        raise RuntimeError("unreachable")
    except _StopEmission as e:
        nc, cx = e.args
        cx.close()
        nc.finalize()
        return nc


def _build_program_inner():
    nc = bass.Bass("TRN2", target_bir_lowering=False,
                   detect_race_conditions=not os.environ.get("BASS_MK_NORACE"))
    cx = Ctx(nc)
    _uid = [0]

    def nm(base):
        _uid[0] += 1
        return f"{base}{_uid[0]}"

    ap = {}
    def ein(name, shape, dtype=F32):
        ap[name] = nc.dram_tensor(name, list(shape), dtype,
                                  kind="ExternalInput").ap()
    ein("x_sp", (D, T))
    ein("cos_t", (T, 1024)); ein("sin_t", (T, 1024))
    ein("wcol_t", (1, 4), I32); ein("qp_t", (128, F))
    ein("pen2_t", (128, 64)); ein("bmask_t", (128, 8 * 512))
    ein("penb_t", (128, 8 * 512))
    ein("parity", (1, 1), I32)
    ein("w_in_q", (KC, 128, D)); ein("w_in_kv", (8, KC, 128, 512))
    ein("w_out_a", (KC, 128, D))
    ein("w_qkv_qk", (2 * KC, 128, D)); ein("w_qkv_v", (4, KC, 128, 512))
    ein("w_outb", (KC, 128, D))
    ein("w_router", (D, NE))
    ein("we1b", (NE, 8, 128, D)); ein("we3b", (NE, 8, 128, D))
    ein("we2r", (NE, 8, 128, D))
    ein("ws1b", (32, 128, D)); ein("ws3b", (32, 128, D)); ein("ws2r", (32, 128, D))
    ein("b_in", (1, 3 * D)); ein("b_out_a", (1, D))
    ein("b_qkv", (1, 3 * D)); ein("b_outb", (1, D))
    ein("gb_cols", (128, 2 * KC))
    ein("identity", (128, 128))
    ein("magic", (1, 4))

    out = nc.dram_tensor("out", [D, F], F32, kind="ExternalOutput").ap()
    x_str = nc.dram_tensor("x_str", [D, T], F32, kind="Internal").ap()
    kt_priv = nc.dram_tensor("kt_priv", [D, 2 * T], BF16, kind="Internal").ap()
    v_priv = nc.dram_tensor("v_priv", [T, D], BF16, kind="Internal").ap()
    q_str = nc.dram_tensor("q_str", [D, F], F32, kind="Internal").ap()
    _shr = dict(addr_space="Shared") if not NOBAR else {}
    sh_kt = nc.dram_tensor("sh_kt", [2, D, 2 * T], BF16, kind="Internal",
                           **_shr).ap()
    sh_v = nc.dram_tensor("sh_v", [2, T, D], BF16, kind="Internal",
                          **_shr).ap()
    sh_a = nc.dram_tensor("sh_a", [2, 128, KC], F32, kind="Internal",
                          **_shr).ap()
    sh_flag = nc.dram_tensor("sh_flag", [2, 2, 4], F32, kind="Internal",
                             **_shr).ap()

    QSEMS = ("SQ", "AQ", "GQ", "LD", "ST0", "ST1", "LNL", "WQ0", "WQ1",
             "KV0", "KV1", "W13A", "W13B", "W2S", "TRG", "EXC", "CWR",
             "KH0", "KH1", "KTS", "EXS")
    for s in QSEMS + ("PE", "DV", "AC", "PO", "PL"):
        cx.sem(s)
    V = Chained(cx, nc.vector, "DV")
    A = Chained(cx, nc.scalar, "AC")
    G = Chained(cx, nc.gpsimd, "PO")

    def pe_tick():
        return cx.tick(nc.tensor.drain(), "PE")

    def fence():
        pe_tick()
        for eng in ("sync", "scalar", "vector", "tensor", "gpsimd"):
            for s in ("PE", "DV", "AC", "PO") + QSEMS:
                cx.wait(eng, s)

    def dev_tap(name, srcv, n=F, conv=None):
        if TAP != name:
            return
        fence()
        if conv == "bf16":
            with nc.sbuf_tensor(nm("tapc"), [128, 1024], F32) as tc_:
                for kc in range(KC):
                    for sl in range(0, n, 1024):
                        w = min(1024, n - sl)
                        cx.wait("vector", "SQ")
                        V.tensor_copy(tc_[:, 0:w], srcv(kc)[:, sl:sl + w])
                        cx.wait("sync", "DV")
                        cx.dma("sync", "SQ", out[kc * 128:(kc + 1) * 128,
                                                 sl:sl + w], tc_[:, 0:w])
        else:
            for kc in range(KC):
                cx.dma("sync", "SQ", out[kc * 128:(kc + 1) * 128, 0:n], srcv(kc))
        cx.wait("sync", "SQ")
        cx.wait("gpsimd", "SQ")
        raise _StopEmission(nc, cx)

    sb_ones_col = nc.alloc_sbuf_tensor("ones_col", [128, 8], BF16)
    sb_ones_row = nc.alloc_sbuf_tensor("ones_row", [1, 1024], BF16)
    sb_gb = nc.alloc_sbuf_tensor("gb", [128, 2 * KC], F32)
    sb_id = nc.alloc_sbuf_tensor("idm", [128, 128], F32)
    sb_wcol = nc.alloc_sbuf_tensor("wcol", [1, 4], I32)
    sb_magic = nc.alloc_sbuf_tensor("sb_magic", [1, 4], F32)
    sb_flag = nc.alloc_sbuf_tensor("flag", [1, 4], F32)
    sb_par = nc.alloc_sbuf_tensor("par", [1, 1], I32)
    sb_stats = nc.alloc_sbuf_tensor("stats", [1, 1536], F32)
    sb_rows = nc.alloc_sbuf_tensor("rows", [1, 2048], BF16)
    sb_bias = nc.alloc_sbuf_tensor("biasr", [1, 8192], BF16)
    sb_col = nc.alloc_sbuf_tensor("colscr", [128, 64], F32)
    sb_colb = nc.alloc_sbuf_tensor("colscrb", [128, 64], BF16)
    sb_c128 = nc.alloc_sbuf_tensor("c128", [128, 4], F32)
    sb_c1 = nc.alloc_sbuf_tensor("c1", [1, 4], F32)

    cx.dma("sync", "SQ", sb_gb[:, :], ap["gb_cols"])
    cx.dma("sync", "SQ", sb_id[:, :], ap["identity"])
    cx.dma("sync", "SQ", sb_wcol[:, :], ap["wcol_t"])
    cx.dma("sync", "SQ", sb_magic[:, :], ap["magic"])
    cx.dma("sync", "SQ", sb_par[:, :], ap["parity"])
    for kc in range(KC):
        cx.dma("sync", "SQ", x_str[kc * 128:(kc + 1) * 128, :],
               ap["x_sp"][kc * 128:(kc + 1) * 128, :])
    V.memset(sb_ones_col[:, :], 1.0)
    V.memset(sb_ones_row[:, :], 1.0)
    V.memset(sb_c128[:, 0:1], EPS)
    V.memset(sb_c128[:, 1:2], -0.5)
    V.memset(sb_c1[:, 0:1], EPS)
    V.memset(sb_c1[:, 1:2], -0.5)

    with nc.sbuf_tensor(nm("rowstage"), [1, 8192], F32) as rst:
        cx.dma("sync", "SQ", rst.ap()[0:1, 0:3 * D], ap["b_in"])
        cx.dma("sync", "SQ", rst.ap()[0:1, 3 * D:4 * D], ap["b_out_a"])
        cx.wait("vector", "SQ")
        V.tensor_copy(sb_bias.ap()[0:1, 0:4 * D], rst.ap()[0:1, 0:4 * D])
        fence()

    sv_my = nc.values_load(sb_par.ap()[0:1, 0:1], engines=[ET.Pool],
                           skip_runtime_bounds_check=True)
    sv_peer = nc.gpsimd.compute_val((1 - sv_my) if not NOBAR else sv_my * 1)
    sv_slot_sp = nc.values_load(sb_par.ap()[0:1, 0:1], engines=[ET.SP],
                                skip_runtime_bounds_check=True)
    sv_wcol_sp = [nc.values_load(sb_wcol.ap()[0:1, t:t + 1], engines=[ET.SP],
                                 skip_runtime_bounds_check=True)
                  for t in range(4)]
    OWNER_R = [0, 0, 1, 1, 1, 1, 0, 0]

    def publish_flag(phase):
        if NOBAR:
            return
        cx.wait("gpsimd", "EXC")
        cx.wait("gpsimd", "EXS")
        cx.wait("gpsimd", "GQ")
        cx.dma("gpsimd", "EXC", sh_flag[ds(sv_my * 1, 1), phase, :],
               sb_magic[:, :])

    def poll_flag(phase):
        if NOBAR:
            return
        cx.wait("gpsimd", "DV")
        with nc.gpsimd.register() as r_exp, nc.gpsimd.register() as r_bad:
            nc.gpsimd.reg_mov(r_bad, 1)
            nc.gpsimd.reg_mov(r_exp, cx.cnt["PL"] + 16)
            with nc.gpsimd.While(lambda: r_bad):
                nc.gpsimd.dma_start(
                    sb_flag[:, :], sh_flag[ds(sv_peer * 1, 1), phase, :],
                ).then_inc(cx.h["PL"], 16)
                nc.gpsimd.wait_ge(cx.h["PL"], r_exp)
                nc.gpsimd.reg_add(r_exp, r_exp, 16)
                nc.gpsimd.reg_mov(r_bad, 0)
                for i in range(4):
                    vi = nc.gpsimd.value_load(
                        sb_flag.ap().bitcast(I32)[0:1, i:i + 1])
                    wi = nc.gpsimd.value_load(
                        sb_magic.ap().bitcast(I32)[0:1, i:i + 1])
                    with nc.gpsimd.register() as rt:
                        nc.gpsimd.reg_alu(rt, vi, wi, ALU.not_equal)
                        nc.gpsimd.reg_alu(r_bad, r_bad, rt, ALU.add)
        cx.cnt["PL"] += 16
        G.memset(sb_flag[:, :], 0.0)

    def emit_ln(dst, t0, tn, affine=False, src_bf=None, per_chunk_f32=None,
                per_sl_end=None):
        fence()
        n_sl = tn // TCH
        with nc.sbuf_tensor(nm("ln_x"), [128, KC * TCH], BF16) as lnx, \
             nc.sbuf_tensor(nm("ln_sq"), [128, KC * TCH], BF16) as lnsq, \
             nc.sbuf_tensor(nm("ln_f32"), [128, TCH], F32) as lnf32, \
             nc.psum_tensor(nm("p_s0"), [1, TCH], F32) as p_s0, \
             nc.psum_tensor(nm("p_s1"), [1, TCH], F32) as p_s1, \
             nc.psum_tensor(nm("p_mb"), [128, TCH], F32) as p_mb, \
             nc.psum_tensor(nm("p_rb"), [128, TCH], F32) as p_rb:
            for sl in range(n_sl):
                tb = t0 + sl * TCH
                if src_bf is None:
                    cx.wait("gpsimd", "DV")
                    cx.wait("gpsimd", "PE")
                    for kc in range(KC):
                        cx.dma("gpsimd", "LNL", lnx[:, kc * TCH:(kc + 1) * TCH],
                               x_str[kc * 128:(kc + 1) * 128, tb:tb + TCH])
                    cx.wait("vector", "LNL")
                    cx.wait("tensor", "LNL")
                    src = lambda kc, _sl=sl: lnx[:, kc * TCH:(kc + 1) * TCH]
                else:
                    src = lambda kc, _sl=sl: src_bf(kc, _sl)
                for kc in range(KC):
                    V.tensor_tensor(lnsq[:, kc * TCH:(kc + 1) * TCH],
                                    src(kc), src(kc), op=ALU.mult)
                cx.wait("tensor", "DV")
                for kc in range(KC):
                    nc.tensor.matmul(p_s0[:, :], sb_ones_col[:, 0:1], src(kc),
                                     start=(kc == 0), stop=(kc == KC - 1))
                for kc in range(KC):
                    t = nc.tensor.matmul(
                        p_s1[:, :], sb_ones_col[:, 0:1],
                        lnsq[:, kc * TCH:(kc + 1) * TCH],
                        start=(kc == 0), stop=(kc == KC - 1))
                pe_tick()
                cx.wait("vector", "PE")
                mrow = sb_stats.ap()[0:1, 0:TCH]
                vrow = sb_stats.ap()[0:1, TCH:2 * TCH]
                scr = sb_stats.ap()[0:1, 2 * TCH:3 * TCH]
                V.tensor_scalar(mrow, p_s0[:, :], 1.0 / D, None, op0=ALU.mult)
                V.tensor_scalar(vrow, p_s1[:, :], 1.0 / D, None, op0=ALU.mult)
                V.tensor_tensor(scr, mrow, mrow, op=ALU.mult)
                V.tensor_tensor(vrow, vrow, scr, op=ALU.subtract)
                cx.wait("scalar", "DV")
                A.activation(vrow, vrow, AF.Ln, bias=sb_c1.ap()[:, 0:1], scale=1.0)
                A.activation(vrow, vrow, AF.Exp, bias=0.0,
                             scale=sb_c1.ap()[:, 1:2])
                cx.wait("vector", "AC")
                m_hi = sb_rows.ap()[0:1, 0:TCH]
                m_lo = sb_rows.ap()[0:1, TCH:2 * TCH]
                r_hi = sb_rows.ap()[0:1, 2 * TCH:3 * TCH]
                r_lo = sb_rows.ap()[0:1, 3 * TCH:4 * TCH]
                for (frow, hi, lo) in ((mrow, m_hi, m_lo), (vrow, r_hi, r_lo)):
                    V.tensor_copy(hi, frow)
                    V.tensor_copy(scr, hi)
                    V.tensor_tensor(scr, frow, scr, op=ALU.subtract)
                    V.tensor_copy(lo, scr)
                cx.wait("tensor", "DV")
                nc.tensor.matmul(p_mb[:, :], sb_ones_row[0:1, 0:128], m_hi,
                                 start=True, stop=False)
                nc.tensor.matmul(p_mb[:, :], sb_ones_row[0:1, 0:128], m_lo,
                                 start=False, stop=True)
                nc.tensor.matmul(p_rb[:, :], sb_ones_row[0:1, 0:128], r_hi,
                                 start=True, stop=False)
                t = nc.tensor.matmul(p_rb[:, :], sb_ones_row[0:1, 0:128], r_lo,
                                     start=False, stop=True)
                pe_tick()
                cx.wait("vector", "PE")
                for kc in range(KC):
                    V.tensor_tensor(lnf32[:, :], src(kc), p_mb[:, :],
                                    op=ALU.subtract)
                    V.tensor_tensor(lnf32[:, :], lnf32[:, :], p_rb[:, :],
                                    op=ALU.mult)
                    if affine:
                        V.tensor_scalar(
                            dst(kc, sl), lnf32[:, :], sb_gb.ap()[:, kc:kc + 1],
                            sb_gb.ap()[:, KC + kc:KC + kc + 1],
                            op0=ALU.mult, op1=ALU.add)
                    else:
                        V.tensor_copy(dst(kc, sl), lnf32[:, :])
                    if per_chunk_f32 is not None:
                        cx.wait("tensor", "DV")
                        per_chunk_f32(sl, kc, lnf32)
                        cx.wait("vector", "PE")
                if per_sl_end is not None:
                    per_sl_end(sl)
            fence()

    def emit_rmw(t0, tn, add_fn, dst_dram=None):
        fence()
        dst_dram = x_str if dst_dram is None else dst_dram
        n_sl = tn // TCH
        with nc.sbuf_tensor(nm("rmw"), [128, 2 * TCH], F32) as rmw:
            for i in range(n_sl * KC):
                sl, kc = divmod(i, KC)
                tb = t0 + sl * TCH
                buf = i % 2
                stn = "ST0" if buf == 0 else "ST1"
                xa = rmw[:, buf * TCH:(buf + 1) * TCH]
                cx.wait("sync", stn)
                cx.dma("sync", "LD", xa,
                       x_str[kc * 128:(kc + 1) * 128, tb:tb + TCH])
                cx.wait("vector", "LD")
                add_fn(kc, sl, xa)
                cx.wait("sync", "DV")
                cx.dma("sync", stn,
                       dst_dram[kc * 128:(kc + 1) * 128, tb:tb + TCH], xa)
            fence()

    def emit_fm_proj(wdram, nb, rhs, tn, bias_off, evac):
        fence()
        n_tsl = tn // TCH
        with nc.sbuf_tensor(nm("wq"), [128, 2 * KC * 128], BF16) as wq, \
             nc.psum_tensor(nm("pp0"), [128, TCH], F32) as pp0, \
             nc.psum_tensor(nm("pp1"), [128, TCH], F32) as pp1:
            pps = [pp0, pp1]
            pe_done = {}
            evt_dv = {}
            evt_ac = {}
            for blk in range(nb):
                slot = blk % 2
                wsem = "WQ0" if slot == 0 else "WQ1"
                if blk >= 2:
                    cx.wait("gpsimd", "PE", pe_done[blk - 2])
                cx.dma("gpsimd", wsem,
                       wq[:, slot * KC * 128:(slot + 1) * KC * 128],
                       ap[wdram][blk])
                cx.wait("tensor", wsem)
                for tsl in range(n_tsl):
                    gi = blk * n_tsl + tsl
                    p = pps[gi % 2]
                    if gi >= 2:
                        cx.wait("tensor", "DV", evt_dv.get(gi - 2, 0))
                        cx.wait("tensor", "AC", evt_ac.get(gi - 2, 0))
                    first = True
                    if bias_off is not None:
                        nc.tensor.matmul(
                            p[:, :], sb_bias.ap()[0:1, bias_off + blk * 128:
                                                  bias_off + (blk + 1) * 128],
                            sb_ones_row[0:1, 0:TCH], start=True, stop=False)
                        first = False
                    for kc in range(KC):
                        t = nc.tensor.matmul(
                            p[:, :],
                            wq[:, slot * KC * 128 + kc * 128:
                               slot * KC * 128 + (kc + 1) * 128],
                            rhs(kc, tsl), start=(first and kc == 0),
                            stop=(kc == KC - 1))
                    pe_done[blk] = pe_tick()
                    evac(blk, tsl, p)
                    evt_dv[gi] = cx.cnt["DV"]
                    evt_ac[gi] = cx.cnt["AC"]
            fence()

    with nc.sbuf_tensor("xn_t", [128, KC * T], BF16) as xn_t:
        xn = lambda kc, sl: xn_t[:, kc * T + sl * TCH: kc * T + (sl + 1) * TCH]
        xn_tc = lambda kc, tc: xn_t[:, kc * T + tc * 128: kc * T + (tc + 1) * 128]

        with nc.sbuf_tensor("ln0_t", [128, KC * T], BF16) as ln0_t:
            ln0 = lambda kc, sl: ln0_t[:, kc * T + sl * TCH:
                                       kc * T + (sl + 1) * TCH]
            dev_tap("x", lambda kc: x_str[kc * 128:(kc + 1) * 128, 0:F])
            emit_ln(ln0, 0, T, affine=False)
            dev_tap("ln0", lambda kc: ln0_t[:, kc * T: kc * T + F], conv="bf16")
            emit_ln(xn, 0, T, affine=True, src_bf=ln0)
            dev_tap("xn", lambda kc: xn_t[:, kc * T: kc * T + F], conv="bf16")
            emit_rmw(0, T, lambda kc, sl, xa: V.tensor_tensor(
                xa, xa, ln0_t[:, kc * T + sl * TCH: kc * T + (sl + 1) * TCH],
                op=ALU.add))
            dev_tap("lnadd", lambda kc: x_str[kc * 128:(kc + 1) * 128, 0:F])

        fence()
        with nc.sbuf_tensor(nm("khat"), [128, KC * T], BF16) as khat, \
             nc.sbuf_tensor(nm("wkv"), [128, 2 * KC * TCH], BF16) as wkv, \
             nc.sbuf_tensor(nm("kvscr"), [128, 2 * TCH], F32) as kvscr, \
             nc.sbuf_tensor(nm("trig"), [128, 3 * TCH], F32) as trig, \
             nc.sbuf_tensor(nm("ssq"), [128, 3 * KC], F32) as ssq, \
             nc.psum_tensor(nm("pkv0"), [128, TCH], F32) as pkv0, \
             nc.psum_tensor(nm("pkv1"), [128, TCH], F32) as pkv1, \
             nc.psum_tensor(nm("p_a"), [128, KC], F32) as p_a:
            pkvs = [pkv0, pkv1]
            V.memset(ssq[:, :], 0.0)
            pe_done = {}
            evt_dv = {}
            first_amm = [True]
            for half in range(2):
                for nsh in range(4):
                    ns = half * 4 + nsh
                    slot = ns % 2
                    wsem = "KV0" if slot == 0 else "KV1"
                    if ns >= 2:
                        cx.wait("gpsimd", "PE", pe_done[ns - 2])
                    cx.dma("gpsimd", wsem,
                           wkv[:, slot * KC * TCH:(slot + 1) * KC * TCH]
                           .rearrange("p (a n) -> p a n", a=KC),
                           ap["w_in_kv"][ns].rearrange("a p n -> p a n"))
                    cx.wait("tensor", wsem)
                    rot = nsh < 2
                    for tc in range(KC):
                        gi = ns * KC + tc
                        p = pkvs[gi % 2]
                        if gi >= 2:
                            cx.wait("tensor", "DV", evt_dv.get(gi - 2, 0))
                        boff = D + ns * TCH
                        nc.tensor.matmul(
                            p[:, :], sb_ones_row[0:1, 0:128],
                            sb_bias.ap()[0:1, boff:boff + TCH],
                            start=True, stop=False)
                        for kc in range(KC):
                            t = nc.tensor.matmul(
                                p[:, :], xn_tc(kc, tc),
                                wkv[:, slot * KC * TCH + kc * TCH:
                                    slot * KC * TCH + (kc + 1) * TCH],
                                start=False, stop=(kc == KC - 1))
                        pe_done[ns] = pe_tick()
                        scr = kvscr[:, (gi % 2) * TCH:(gi % 2 + 1) * TCH]
                        cx.wait("vector", "PE")
                        V.tensor_copy(scr, p[:, :])
                        if rot:
                            cb = trig[:, 0:TCH]
                            sbn = trig[:, TCH:2 * TCH]
                            swp = trig[:, 2 * TCH:3 * TCH]
                            cx.wait("sync", "DV")
                            cx.dma("sync", "TRG", cb,
                                   ap["cos_t"][tc * 128:(tc + 1) * 128,
                                               nsh * TCH:(nsh + 1) * TCH])
                            cx.dma("sync", "TRG", sbn,
                                   ap["sin_t"][tc * 128:(tc + 1) * 128,
                                               nsh * TCH:(nsh + 1) * TCH])
                            cx.wait("vector", "TRG")
                            s3 = scr.rearrange("p (a b) -> p a b", b=2)
                            w3 = swp.rearrange("p (a b) -> p a b", b=2)
                            V.tensor_copy(w3[:, :, 0:1], s3[:, :, 1:2])
                            V.tensor_copy(w3[:, :, 1:2], s3[:, :, 0:1])
                            V.tensor_tensor(swp, swp, sbn, op=ALU.mult)
                            V.tensor_tensor(scr, scr, cb, op=ALU.mult)
                            V.tensor_tensor(scr, scr, swp, op=ALU.add)
                        sq = trig[:, 2 * TCH:3 * TCH]
                        V.tensor_tensor(sq, scr, scr, op=ALU.mult)
                        red = ssq[:, 2 * KC + tc:2 * KC + tc + 1]
                        V.reduce_sum(red, sq, axis=mybir.AxisListType.X)
                        acc = ssq[:, half * KC + tc:half * KC + tc + 1]
                        V.tensor_tensor(acc, acc, red, op=ALU.add)
                        if half == 0:
                            V.tensor_copy(
                                khat[:, tc * D + ns * TCH:
                                     tc * D + (ns + 1) * TCH], scr)
                        else:
                            kslice = khat[:, tc * D + nsh * TCH:
                                          tc * D + (nsh + 1) * TCH]
                            V.tensor_tensor(kslice, kslice, scr, op=ALU.mult)
                        evt_dv[gi] = cx.cnt["DV"]
                if half == 0:
                    cx.wait("scalar", "DV")
                    rn = ssq[:, 0:KC]
                    A.activation(rn, rn, AF.Ln, bias=0.0, scale=1.0)
                    A.activation(rn, rn, AF.Exp, bias=0.0,
                                 scale=sb_c128.ap()[:, 1:2])
                    cx.wait("vector", "AC")
                    for tc in range(KC):
                        for q in range(4):
                            V.tensor_scalar(
                                khat[:, tc * D + q * TCH: tc * D + (q + 1) * TCH],
                                khat[:, tc * D + q * TCH: tc * D + (q + 1) * TCH],
                                ssq[:, tc:tc + 1], None, op0=ALU.mult)
                    cx.wait("tensor", "DV")
            cx.wait("scalar", "DV")
            rnv = ssq[:, KC:2 * KC]
            A.activation(rnv, rnv, AF.Ln, bias=0.0, scale=1.0)
            A.activation(rnv, rnv, AF.Exp, bias=0.0, scale=sb_c128.ap()[:, 1:2])
            cx.wait("vector", "AC")
            V.tensor_copy(sb_colb[:, 0:KC], rnv)
            cx.wait("tensor", "DV")
            for tc in range(KC):
                for j in range(KC):
                    t = nc.tensor.matmul(
                        p_a[:, j:j + 1],
                        khat[:, tc * D + j * 128: tc * D + (j + 1) * 128],
                        sb_colb[:, tc:tc + 1],
                        start=first_amm[0], stop=(tc == KC - 1 and j == KC - 1))
                    first_amm[0] = False
            pe_tick()
            cx.wait("vector", "PE")
            V.tensor_copy(sb_col[:, 0:KC], p_a[:, :])
            if TAP == "apart":
                cx.wait("sync", "DV")
                cx.dma("sync", "SQ", out[0:128, 0:KC], sb_col[:, 0:KC])
                cx.wait("sync", "SQ")
                raise _StopEmission(nc, cx)
            cx.wait("gpsimd", "DV")
            cx.dma("gpsimd", "EXC", sh_a[ds(sv_my * 1, 1), :, :]
                   .rearrange("a p n -> p a n"),
                   sb_col[:, 0:KC].rearrange("p (a n) -> p a n", a=1))
            publish_flag(0)
            fence()

        with nc.sbuf_tensor(nm("qt"), [128, KC * T], BF16) as qt_t:
            qt = lambda kc, sl: qt_t[:, kc * T + sl * TCH:
                                     kc * T + (sl + 1) * TCH]

            def evac_q(blk, tsl, p):
                cx.wait("scalar", "PE")
                A.copy(qt(blk, tsl), p[:, :])

            emit_fm_proj("w_in_q", KC, xn, T, 0, evac_q)
            dev_tap("q", lambda kc: qt_t[:, kc * T: kc * T + F], conv="bf16")

            poll_flag(0)
            cx.wait("gpsimd", "PO")
            cx.wait("gpsimd", "EXC")
            cx.dma("gpsimd", "EXC",
                   sb_col[:, KC:2 * KC].rearrange("p (a n) -> p a n", a=1),
                   sh_a[ds(sv_peer * 1, 1), :, :].rearrange("a p n -> p a n"))
            cx.wait("vector", "EXC")
            V.tensor_tensor(sb_col[:, 0:KC], sb_col[:, 0:KC],
                            sb_col[:, KC:2 * KC], op=ALU.add)
            V.tensor_scalar(sb_col[:, 0:KC], sb_col[:, 0:KC],
                            0.5, 0.5, op0=ALU.mult, op1=ALU.add)
            cx.wait("vector", "AC")
            for blk in range(KC):
                for tsl in range(T // TCH):
                    V.tensor_scalar(qt(blk, tsl), qt(blk, tsl),
                                    sb_col[:, blk:blk + 1], None, op0=ALU.mult)
            cx.wait("tensor", "DV")
            dev_tap("qs", lambda kc: qt_t[:, kc * T: kc * T + F], conv="bf16")

            with nc.sbuf_tensor(nm("rmw2"), [128, 2 * TCH], F32) as rmw2:

                def evac_out(blk, tsl, p):
                    gi = blk * (T // TCH) + tsl
                    buf = gi % 2
                    stn = "ST0" if buf == 0 else "ST1"
                    xa = rmw2[:, buf * TCH:(buf + 1) * TCH]
                    cx.wait("sync", stn)
                    cx.dma("sync", "LD", xa,
                           x_str[blk * 128:(blk + 1) * 128,
                                 tsl * TCH:(tsl + 1) * TCH])
                    cx.wait("vector", "LD")
                    cx.wait("vector", "PE")
                    V.tensor_tensor(xa, xa, p[:, :], op=ALU.add)
                    cx.wait("sync", "DV")
                    cx.dma("sync", stn,
                           x_str[blk * 128:(blk + 1) * 128,
                                 tsl * TCH:(tsl + 1) * TCH], xa)

                emit_fm_proj("w_out_a", KC, qt, T, 3 * D, evac_out)
                fence()

    def dump_and_stop():
        fence()
        for kc in range(KC):
            cx.dma("sync", "SQ", out[kc * 128:(kc + 1) * 128, :],
                   x_str[kc * 128:(kc + 1) * 128, 0:F])
        cx.wait("sync", "SQ")
        cx.wait("gpsimd", "SQ")
        raise _StopEmission(nc, cx)

    if N_PHASES <= 1:
        dump_and_stop()

    sb_wr = nc.alloc_sbuf_tensor("sb_wr", [128, KC * NE], F32)
    cx.dma("sync", "SQ",
           sb_wr[:, :].rearrange("p (a n) -> p a n", a=KC),
           ap["w_router"].rearrange("(a p) n -> p a n", p=128))
    cx.wait("tensor", "SQ")
    cx.wait("vector", "SQ")

    def emit_ffn(tn, tap_prefix=""):
        n_half = tn // 1024
        for th in range(n_half):
            t0 = th * 1024
            with nc.sbuf_tensor(nm("ffln"), [128, KC * 1024], BF16) as ln_t, \
                 nc.sbuf_tensor(nm("cwT"), [16, 1024], F32) as cwT, \
                 nc.sbuf_tensor(nm("cwrow"), [1, 2 * 1024], BF16) as cwrow, \
                 nc.sbuf_tensor(nm("cwst"), [1, 1024], F32) as cwst, \
                 nc.sbuf_tensor(nm("prb"), [128, 2 * NE], F32) as prb:
                lnv = lambda kc, sl: ln_t[:, kc * 1024 + sl * TCH:
                                          kc * 1024 + (sl + 1) * TCH]

                with nc.psum_tensor(nm("p_rt"), [16, TCH], F32) as p_rt, \
                     nc.psum_tensor(nm("p_tt"), [128, NE], F32) as p_tt, \
                     nc.psum_tensor(nm("p_cwt"), [16, 128], F32) as p_cwt, \
                     nc.sbuf_tensor(nm("rtf"), [16, TCH], F32) as rtf:

                    def rt_hook(sl, kc, lnf32):
                        nc.tensor.matmul(
                            p_rt[:, :], sb_wr[:, kc * NE:(kc + 1) * NE],
                            lnf32[:, :], start=(kc == 0), stop=(kc == KC - 1))
                        pe_tick()

                    def rt_end(sl):
                        cx.wait("vector", "PE")
                        V.tensor_copy(rtf.ap()[0:16, :], p_rt[0:16, :])
                        if TAP == tap_prefix + "rt" and sl == 0:
                            cx.wait("sync", "DV")
                            cx.dma("sync", "SQ", out[0:16, 0:TCH],
                                   rtf.ap()[0:16, :])
                            cx.wait("sync", "SQ")
                            raise _StopEmission(nc, cx)
                        for tt in range(4):
                            cx.wait("tensor", "DV")
                            nc.tensor.transpose(
                                p_tt[:, :], rtf.ap()[0:16, tt * 128:(tt + 1) * 128],
                                sb_id[0:16, 0:16])
                            pe_tick()
                            cx.wait("vector", "PE")
                            pslice = p_tt[:, 0:NE]
                            rmax = sb_col[:, 32:33]
                            V.reduce_max(rmax, pslice,
                                         axis=mybir.AxisListType.X, negate=True)
                            cx.wait("scalar", "DV")
                            A.activation(prb[:, 0:NE], pslice, AF.Exp,
                                         bias=rmax, scale=1.0)
                            cx.wait("vector", "AC")
                            wk = prb[:, NE:2 * NE]
                            V.tensor_copy(wk, prb[:, 0:NE])
                            for it in range(3):
                                mx = sb_col[:, 33:34]
                                V.reduce_max(mx, wk, axis=mybir.AxisListType.X)
                                eq = sb_col[:, 34:50]
                                V.tensor_scalar(eq, wk, mx, None, op0=ALU.is_ge)
                                V.scalar_tensor_tensor(wk, eq, -1e30, wk,
                                                       op0=ALU.mult, op1=ALU.add)
                            m4 = sb_col[:, 33:34]
                            V.reduce_max(m4, wk, axis=mybir.AxisListType.X)
                            sel = sb_col[:, 34:50]
                            V.tensor_scalar(sel, prb[:, 0:NE], m4, None,
                                            op0=ALU.is_ge)
                            V.tensor_tensor(wk, prb[:, 0:NE], sel, op=ALU.mult)
                            ssum = sb_col[:, 33:34]
                            V.reduce_sum(ssum, wk, axis=mybir.AxisListType.X)
                            V.reciprocal(ssum, ssum)
                            V.tensor_scalar(wk, wk, ssum, 0.8,
                                            op0=ALU.mult, op1=ALU.mult)
                            cx.wait("tensor", "DV")
                            nc.tensor.transpose(p_cwt[:, :], wk, sb_id[:, :])
                            pe_tick()
                            cx.wait("vector", "PE")
                            V.tensor_copy(
                                cwT.ap()[0:16, sl * TCH + tt * 128:
                                         sl * TCH + (tt + 1) * 128],
                                p_cwt[0:16, :])
                        cx.wait("tensor", "DV")

                    emit_ln(lnv, t0, 1024, affine=False,
                            per_chunk_f32=rt_hook, per_sl_end=rt_end)

                fence()

                if TAP == tap_prefix + "cw":
                    with nc.sbuf_tensor(nm("cwo"), [128, 1024], F32) as cwo:
                        V.memset(cwo[:, :], 0.0)
                        V.tensor_copy(cwo[0:16, :], cwT.ap()[0:16, :])
                        cx.wait("sync", "DV")
                        cx.dma("sync", "SQ", out[0:128, 0:1024], cwo[:, :])
                        cx.wait("sync", "SQ")
                        raise _StopEmission(nc, cx)

                with nc.sbuf_tensor(nm("moe"), [128, KC * 1024], F32) as moe, \
                     ExitStack() as _es:
                    hT = _es.enter_context(
                        nc.sbuf_tensor(nm("hT"), [128, 8 * 1024], BF16))
                    w13 = _es.enter_context(
                        nc.sbuf_tensor(nm("w13"), [128, 4 * KC * 128], BF16))
                    w2 = _es.enter_context(
                        nc.sbuf_tensor(nm("w2"), [128, 8 * D], BF16))
                    gsc = _es.enter_context(
                        nc.sbuf_tensor(nm("gsc"), [128, 2 * TCH], BF16))
                    ph1a = _es.enter_context(
                        nc.psum_tensor(nm("ph1a"), [128, TCH], F32))
                    ph3a = _es.enter_context(
                        nc.psum_tensor(nm("ph3a"), [128, TCH], F32))
                    ph1b = _es.enter_context(
                        nc.psum_tensor(nm("ph1b"), [128, TCH], F32))
                    ph3b = _es.enter_context(
                        nc.psum_tensor(nm("ph3b"), [128, TCH], F32))
                    pcw = _es.enter_context(
                        nc.psum_tensor(nm("pcw"), [128, TCH], F32))
                    pm2a = _es.enter_context(
                        nc.psum_tensor(nm("pm2a"), [128, TCH], F32))
                    pm2b = _es.enter_context(
                        nc.psum_tensor(nm("pm2b"), [128, TCH], F32))
                    ph1 = [ph1a, ph1b]
                    ph3 = [ph3a, ph3b]
                    pm2 = [pm2a, pm2b]
                    w13_done = {}
                    mm1_cnt = [0]
                    mm2_cnt = [0]
                    units = ([("e", e) for e in range(NE)]
                             + [("s", g) for g in range(4)])
                    for ui, (kind, idx) in enumerate(units):
                        if kind == "e":
                            cx.wait("sync", "PE")
                            cx.wait("sync", "DV")
                            cx.dma("sync", "CWR", cwst.ap()[0:1, :],
                                   cwT.ap()[idx:idx + 1, :])
                            cx.wait("vector", "CWR")
                            V.tensor_copy(cwrow.ap()[0:1, 0:1024],
                                          cwst.ap()[0:1, :])
                            V.tensor_copy(sb_stats.ap()[0:1, 0:1024],
                                          cwrow.ap()[0:1, 0:1024])
                            V.tensor_tensor(sb_stats.ap()[0:1, 0:1024],
                                            cwst.ap()[0:1, :],
                                            sb_stats.ap()[0:1, 0:1024],
                                            op=ALU.subtract)
                            V.tensor_copy(cwrow.ap()[0:1, 1024:2048],
                                          sb_stats.ap()[0:1, 0:1024])
                        for hc in range(8):
                            seq = ui * 8 + hc
                            slot = seq % 2
                            if seq >= 2:
                                cx.wait("gpsimd", "PE", w13_done[seq - 2])
                            if kind == "e":
                                src1 = ap["we1b"][idx, hc]
                                src3 = ap["we3b"][idx, hc]
                            else:
                                src1 = ap["ws1b"][idx * 8 + hc]
                                src3 = ap["ws3b"][idx * 8 + hc]
                            o1 = slot * 2 * KC * 128
                            o3 = slot * 2 * KC * 128 + KC * 128
                            wsem = "W13A" if slot == 0 else "W13B"
                            cx.dma("gpsimd", wsem, w13[:, o1:o1 + KC * 128], src1)
                            cx.dma("gpsimd", wsem, w13[:, o3:o3 + KC * 128], src3)
                            cx.wait("tensor", wsem)
                            for ts2 in range(2):
                                gi = mm1_cnt[0]; mm1_cnt[0] += 1
                                p1 = ph1[gi % 2]
                                p3 = ph3[gi % 2]
                                cx.wait("tensor", "DV")
                                for kc in range(KC):
                                    nc.tensor.matmul(
                                        p1[:, :], w13[:, o1 + kc * 128:o1 + (kc + 1) * 128],
                                        lnv(kc, ts2), start=(kc == 0),
                                        stop=(kc == KC - 1))
                                for kc in range(KC):
                                    nc.tensor.matmul(
                                        p3[:, :], w13[:, o3 + kc * 128:o3 + (kc + 1) * 128],
                                        lnv(kc, ts2), start=(kc == 0),
                                        stop=(kc == KC - 1))
                                if kind == "e":
                                    nc.tensor.matmul(
                                        pcw[:, :], sb_ones_row[0:1, 0:128],
                                        cwrow.ap()[0:1, ts2 * TCH:(ts2 + 1) * TCH],
                                        start=True, stop=False)
                                    nc.tensor.matmul(
                                        pcw[:, :], sb_ones_row[0:1, 0:128],
                                        cwrow.ap()[0:1, 1024 + ts2 * TCH:
                                                   1024 + (ts2 + 1) * TCH],
                                        start=False, stop=True)
                                w13_done[seq] = pe_tick()
                                cx.wait("scalar", "PE")
                                s1 = gsc[:, 0:TCH]
                                A.activation(s1, p1[:, :], AF.Silu)
                                cx.wait("vector", "AC")
                                g2 = gsc[:, TCH:2 * TCH]
                                V.tensor_tensor(g2, s1, p3[:, :], op=ALU.mult)
                                hslice = hT[:, hc * 1024 + ts2 * TCH:
                                            hc * 1024 + (ts2 + 1) * TCH]
                                if kind == "e":
                                    V.tensor_tensor(hslice, g2, pcw[:, :],
                                                    op=ALU.mult)
                                else:
                                    V.tensor_copy(hslice, g2)
                        if kind == "e":
                            w2src = ap["we2r"][idx]
                        else:
                            w2src = ap["ws2r"][idx * 8:(idx + 1) * 8]
                        cx.wait("gpsimd", "PE")
                        cx.dma("gpsimd", "W2S",
                               w2[:, :].rearrange("p (a n) -> p a n", a=8),
                               w2src.rearrange("a p n -> p a n"))
                        cx.wait("tensor", "W2S")
                        cx.wait("tensor", "DV")
                        for dc in range(KC):
                            for ts2 in range(2):
                                gi = mm2_cnt[0]; mm2_cnt[0] += 1
                                p = pm2[gi % 2]
                                cx.wait("tensor", "DV")
                                for hk in range(8):
                                    nc.tensor.matmul(
                                        p[:, :],
                                        w2[:, hk * D + dc * 128:
                                           hk * D + (dc + 1) * 128],
                                        hT[:, hk * 1024 + ts2 * TCH:
                                           hk * 1024 + (ts2 + 1) * TCH],
                                        start=(hk == 0), stop=(hk == 7))
                                pe_tick()
                                cx.wait("vector", "PE")
                                mslice = moe[:, dc * 1024 + ts2 * TCH:
                                             dc * 1024 + (ts2 + 1) * TCH]
                                if ui == 0:
                                    V.tensor_copy(mslice, p[:, :])
                                elif kind == "e":
                                    V.tensor_tensor(mslice, mslice, p[:, :],
                                                    op=ALU.add)
                                else:
                                    V.scalar_tensor_tensor(
                                        mslice, p[:, :], 0.2, mslice,
                                        op0=ALU.mult, op1=ALU.add)
                    fence()
                    _es.close()
                    emit_rmw(t0, 1024, lambda kc, sl, xa: V.tensor_tensor(
                        xa, xa,
                        moe[:, kc * 1024 + sl * TCH: kc * 1024 + (sl + 1) * TCH],
                        op=ALU.add))

    emit_ffn(T, tap_prefix="f1")
    if N_PHASES <= 2:
        dump_and_stop()

    with nc.sbuf_tensor(nm("rowstage"), [1, 8192], F32) as rst:
        fence()
        cx.dma("sync", "SQ", rst.ap()[0:1, 0:3 * D], ap["b_qkv"])
        cx.dma("sync", "SQ", rst.ap()[0:1, 3 * D:4 * D], ap["b_outb"])
        cx.wait("vector", "SQ")
        V.tensor_copy(sb_bias.ap()[0:1, 0:4 * D], rst.ap()[0:1, 0:4 * D])
        fence()

    ap["_wqk_q"] = [ap["w_qkv_qk"][i] for i in range(KC)]
    ap["_wqk_k"] = [ap["w_qkv_qk"][KC + i] for i in range(KC)]

    with nc.sbuf_tensor(nm("otf"), [128, KC * F], BF16) as otf_t:
        otf = lambda kc, sl: otf_t[:, kc * F + sl * TCH: kc * F + (sl + 1) * TCH]

        with nc.sbuf_tensor(nm("bln"), [128, KC * T], BF16) as bln_t:
            bln = lambda kc, sl: bln_t[:, kc * T + sl * TCH:
                                       kc * T + (sl + 1) * TCH]
            bln_tc = lambda kc, tc: bln_t[:, kc * T + tc * 128:
                                          kc * T + (tc + 1) * 128]
            emit_ln(bln, 0, T, affine=False)

            with nc.sbuf_tensor(nm("kstg"), [128, 4 * TCH], BF16) as kstg, \
                 nc.sbuf_tensor(nm("kscr"), [128, 2 * TCH], F32) as kscr:

                def evac_qf(blk, tsl, p):
                    gi = blk * (F // TCH) + tsl
                    buf = gi % 2
                    stn = "ST0" if buf == 0 else "ST1"
                    scr = kscr[:, buf * TCH:(buf + 1) * TCH]
                    cx.wait("vector", stn)
                    cx.wait("vector", "PE")
                    V.tensor_scalar(scr, p[:, :], QK_SCALE, None, op0=ALU.mult)
                    cx.wait("sync", "DV")
                    cx.dma("sync", stn,
                           q_str[blk * 128:(blk + 1) * 128,
                                 tsl * TCH:(tsl + 1) * TCH], scr)

                emit_fm_proj("_wqk_q", KC, bln, F, 0, evac_qf)

                def evac_kt(blk, tsl, p):
                    gi = blk * (T // TCH) + tsl
                    buf = gi % 2
                    stn = "ST0" if buf == 0 else "ST1"
                    scr = kscr[:, buf * TCH:(buf + 1) * TCH]
                    hi = kstg[:, (2 * buf) * TCH:(2 * buf + 1) * TCH]
                    lo = kstg[:, (2 * buf + 1) * TCH:(2 * buf + 2) * TCH]
                    cx.wait("vector", stn)
                    cx.wait("vector", "PE")
                    V.tensor_scalar(scr, p[:, :], QK_SCALE, None, op0=ALU.mult)
                    V.tensor_copy(hi, scr)
                    V.tensor_tensor(scr, scr, hi, op=ALU.subtract)
                    V.tensor_copy(lo, scr)
                    cx.wait("sync", "DV")
                    cx.dma("sync", stn,
                           kt_priv[blk * 128:(blk + 1) * 128,
                                   tsl * TCH:(tsl + 1) * TCH], hi)
                    cx.dma("sync", stn,
                           kt_priv[blk * 128:(blk + 1) * 128,
                                   T + tsl * TCH: T + (tsl + 1) * TCH], lo)

                emit_fm_proj("_wqk_k", KC, bln, T, D, evac_kt)

                cx.wait("sync", "ST0")
                cx.wait("sync", "ST1")
                for tsl in range(4):
                    if NOBAR:
                        cx.dma("sync", "EXS",
                               sh_kt[0:1, 0:D, ds(sv_wcol_sp[tsl] * 1, TCH)],
                               kt_priv[:, tsl * TCH:(tsl + 1) * TCH]
                               .rearrange("(a p) n -> a p n", a=1))
                        cx.dma("sync", "EXS",
                               sh_kt[0:1, 0:D,
                                     ds((sv_wcol_sp[tsl] + T) * 1, TCH)],
                               kt_priv[:, T + tsl * TCH: T + (tsl + 1) * TCH]
                               .rearrange("(a p) n -> a p n", a=1))
                        continue
                    cx.dma("sync", "EXS",
                           sh_kt[ds(sv_slot_sp * 1, 1), 0:D,
                                 ds(sv_wcol_sp[tsl] * 1, TCH)],
                           kt_priv[:, tsl * TCH:(tsl + 1) * TCH]
                           .rearrange("(a p) n -> a p n", a=1))
                    cx.dma("sync", "EXS",
                           sh_kt[ds(sv_slot_sp * 1, 1), 0:D,
                                 ds((sv_wcol_sp[tsl] + T) * 1, TCH)],
                           kt_priv[:, T + tsl * TCH: T + (tsl + 1) * TCH]
                           .rearrange("(a p) n -> a p n", a=1))

            with nc.sbuf_tensor(nm("vw"), [128, 2 * KC * TCH], BF16) as vw, \
                 nc.sbuf_tensor(nm("vstg"), [128, 2 * TCH], BF16) as vstg, \
                 nc.psum_tensor(nm("pv0"), [128, TCH], F32) as pv0, \
                 nc.psum_tensor(nm("pv1"), [128, TCH], F32) as pv1:
                pvs = [pv0, pv1]
                pe_done = {}
                evt_dv = {}
                for ns in range(4):
                    slot = ns % 2
                    wsem = "KV0" if slot == 0 else "KV1"
                    if ns >= 2:
                        cx.wait("gpsimd", "PE", pe_done[ns - 2])
                    cx.dma("gpsimd", wsem,
                           vw[:, slot * KC * TCH:(slot + 1) * KC * TCH]
                           .rearrange("p (a n) -> p a n", a=KC),
                           ap["w_qkv_v"][ns].rearrange("a p n -> p a n"))
                    cx.wait("tensor", wsem)
                    for tc in range(KC):
                        gi = ns * KC + tc
                        p = pvs[gi % 2]
                        if gi >= 2:
                            cx.wait("tensor", "DV", evt_dv.get(gi - 2, 0))
                        boff = 2 * D + ns * TCH
                        nc.tensor.matmul(
                            p[:, :], sb_ones_row[0:1, 0:128],
                            sb_bias.ap()[0:1, boff:boff + TCH],
                            start=True, stop=False)
                        for kc in range(KC):
                            nc.tensor.matmul(
                                p[:, :], bln_tc(kc, tc),
                                vw[:, slot * KC * TCH + kc * TCH:
                                   slot * KC * TCH + (kc + 1) * TCH],
                                start=False, stop=(kc == KC - 1))
                        pe_done[ns] = pe_tick()
                        buf = gi % 2
                        stn = "ST0" if buf == 0 else "ST1"
                        stg = vstg[:, buf * TCH:(buf + 1) * TCH]
                        cx.wait("vector", stn)
                        cx.wait("vector", "PE")
                        V.tensor_copy(stg, p[:, :])
                        cx.wait("sync", "DV")
                        cx.dma("sync", stn,
                               v_priv[tc * 128:(tc + 1) * 128,
                                      ns * TCH:(ns + 1) * TCH], stg)
                        evt_dv[gi] = cx.cnt["DV"]
                cx.wait("sync", "ST0")
                cx.wait("sync", "ST1")
                for tsl in range(4):
                    if NOBAR:
                        cx.dma("sync", "EXS",
                               sh_v[0:1, ds(sv_wcol_sp[tsl] * 1, TCH), 0:D],
                               v_priv[tsl * TCH:(tsl + 1) * TCH, :]
                               .rearrange("(a p) n -> a p n", a=1))
                        continue
                    cx.dma("sync", "EXS",
                           sh_v[ds(sv_slot_sp * 1, 1),
                                ds(sv_wcol_sp[tsl] * 1, TCH), 0:D],
                           v_priv[tsl * TCH:(tsl + 1) * TCH, :]
                           .rearrange("(a p) n -> a p n", a=1))
                fence()
            publish_flag(1)
            fence()

        dev_tap("bq", lambda kc: q_str[kc * 128:(kc + 1) * 128, 0:F])

        poll_flag(1)
        cx.wait("gpsimd", "PO")

        with ExitStack() as _att:
            _e = _att.enter_context
            ktv = _e(nc.sbuf_tensor(nm("ktv"), [128, 2 * 12288], BF16))
            qfl = _e(nc.sbuf_tensor(nm("qfl"), [128, 2 * F], F32))
            qhl_t = _e(nc.sbuf_tensor(nm("qhl"), [128, 2 * 2 * F], BF16))
            kbnd_t = _e(nc.sbuf_tensor(nm("kbnd"), [128, 2 * F], BF16))
            ptb_t = _e(nc.sbuf_tensor(nm("ptb"), [128, 2 * F], BF16))
            ptm_t = _e(nc.sbuf_tensor(nm("ptm"), [128, 2 * F], BF16))
            qps = _e(nc.sbuf_tensor(nm("qps"), [128, F], F32))
            rbf = _e(nc.sbuf_tensor(nm("rbf"), [128, F], F32))
            lrow = _e(nc.sbuf_tensor(nm("lrow"), [1, 2 * F], F32))
            rrow = _e(nc.sbuf_tensor(nm("rrow"), [1, 2 * F], BF16))
            bmk = _e(nc.sbuf_tensor(nm("bmk"), [128, 8 * TCH], BF16))
            pnb = _e(nc.sbuf_tensor(nm("pnb"), [128, 8 * TCH], BF16))
            pen2 = _e(nc.sbuf_tensor(nm("pen2"), [128, 64], F32))
            mstg = _e(nc.sbuf_tensor(nm("mstg"), [128, TCH], F32))
            msc = _e(nc.sbuf_tensor(nm("msc"), [128, TCH], F32))
            mpart = _e(nc.sbuf_tensor(nm("mpart"), [128, 16], F32))
            mrow8 = _e(nc.sbuf_tensor(nm("mrow8"), [8, 2 * 128], BF16))
            m8f = _e(nc.sbuf_tensor(nm("m8f"), [8, 2 * 128], F32))
            nrow = _e(nc.sbuf_tensor(nm("nrow"), [1, 2 * F], BF16))
            p_sc2 = _e(nc.psum_tensor(nm("p_sc2"), [128, 2 * F], F32))
            p_l = _e(nc.psum_tensor(nm("p_l"), [1, F], F32))
            p_ot = _e(nc.psum_tensor(nm("p_ot"), [128, F], F32))
            cx.dma("sync", "SQ", qps[:, :], ap["qp_t"])
            cx.dma("sync", "SQ", pen2.ap()[:, :], ap["pen2_t"])
            for ch in range(8):
                cx.wait("vector", "SQ")
                cx.dma("sync", "SQ", mstg[:, :],
                       ap["bmask_t"][:, ch * TCH:(ch + 1) * TCH])
                cx.wait("vector", "SQ")
                V.tensor_copy(bmk.ap()[:, ch * TCH:(ch + 1) * TCH], mstg[:, :])
                cx.wait("sync", "DV")
                cx.dma("sync", "SQ", mstg[:, :],
                       ap["penb_t"][:, ch * TCH:(ch + 1) * TCH])
                cx.wait("vector", "SQ")
                V.tensor_copy(pnb.ap()[:, ch * TCH:(ch + 1) * TCH], mstg[:, :])
                cx.wait("sync", "DV")
            cx.wait("tensor", "SQ")
            cx.wait("vector", "SQ")
            if TAP == "bsh":
                fence()
                with nc.sbuf_tensor(nm("shdmp"), [128, 2048], BF16) as sd_, \
                     nc.sbuf_tensor(nm("shdmf"), [128, 2048], F32) as sf_:
                    for kc in range(KC):
                        cx.wait("vector", "SQ")
                        cx.dma("sync", "SQ", sd_[:, 0:2048],
                               sh_kt[0, kc * 128:(kc + 1) * 128, 0:2048])
                        cx.wait("vector", "SQ")
                        V.tensor_copy(sf_[:, :], sd_[:, :])
                        cx.wait("sync", "DV")
                        cx.dma("sync", "SQ", out[kc * 128:(kc + 1) * 128, 0:1024],
                               sf_[:, 0:1024])
                        cx.dma("sync", "SQ", out[kc * 128:(kc + 1) * 128, 0:1024]
                               if False else out[kc * 128:(kc + 1) * 128, 0:1024],
                               sf_[:, 1024:2048]) if False else None
                    cx.wait("sync", "SQ")
                raise _StopEmission(nc, cx)
            if TAP == "bsh2":
                fence()
                with nc.sbuf_tensor(nm("shdmp"), [128, 1024], BF16) as sd_, \
                     nc.sbuf_tensor(nm("shdmf"), [128, 1024], F32) as sf_:
                    for kc in range(KC):
                        cx.wait("vector", "SQ")
                        cx.dma("sync", "SQ", sd_[:, 0:1024],
                               sh_kt[0, kc * 128:(kc + 1) * 128, 1024:2048])
                        cx.wait("vector", "SQ")
                        V.tensor_copy(sf_[:, :], sd_[:, :])
                        cx.wait("sync", "DV")
                        cx.dma("sync", "SQ", out[kc * 128:(kc + 1) * 128, 0:1024],
                               sf_[:, 0:1024])
                    cx.wait("sync", "SQ")
                raise _StopEmission(nc, cx)

            ac_evt = {}
            dv_evt = {}
            pe_evt = {}
            kh_done = {}
            qdv_done = {}
            for h in range(N_HEADS):
                if HARDEN & 1:
                    fence()
                slot = h % 2
                hsem = "KH0" if slot == 0 else "KH1"
                kt_h = ktv[:, slot * 12288: slot * 12288 + 4096]
                kl_h = ktv[:, slot * 12288 + 4096: slot * 12288 + 8192]
                v_h = ktv[:, slot * 12288 + 8192: slot * 12288 + 12288]
                kbnd = kbnd_t[:, slot * F:(slot + 1) * F]
                qf_h = qfl[:, slot * F:(slot + 1) * F]
                q_hi = qhl_t[:, slot * 2 * F: slot * 2 * F + F]
                q_lo = qhl_t[:, slot * 2 * F + F: slot * 2 * F + 2 * F]
                if h >= 2:
                    cx.wait("sync", "PE", kh_done[h - 2])
                    cx.wait("sync", "DV", qdv_done[h - 2])
                    cx.wait("vector", "PE", kh_done[h - 2])
                for (slab, c0, c1, s0) in ((0, 0, 1024, 0),
                                           (0 if NOBAR else 1, 1024, 3072, 0),
                                           (0, 3072, 4096, 1024)):
                    w = c1 - c0
                    cx.dma("sync", hsem, kt_h[:, c0:c1],
                           sh_kt[slab, h * 128:(h + 1) * 128, s0:s0 + w])
                    cx.dma("sync", hsem, kl_h[:, c0:c1],
                           sh_kt[slab, h * 128:(h + 1) * 128,
                                 T + s0: T + s0 + w])
                    cx.dma("sync", hsem,
                           v_h[:, c0:c1]
                           .rearrange("p (b n) -> p b n", n=128),
                           sh_v[slab, s0:s0 + w, h * 128:(h + 1) * 128]
                           .rearrange("(b p) n -> p b n", p=128))
                cx.dma("sync", hsem, kbnd,
                       kt_priv[h * 128:(h + 1) * 128, 0:F])
                cx.dma("sync", hsem, qf_h, q_str[h * 128:(h + 1) * 128, :])
                cx.wait("vector", hsem)
                V.tensor_copy(q_hi, qf_h)
                V.tensor_copy(msc[:, :], q_hi[:, 0:TCH])
                V.tensor_tensor(msc[:, :], qf_h[:, 0:TCH], msc[:, :],
                                op=ALU.subtract)
                V.tensor_copy(q_lo[:, 0:TCH], msc[:, :])
                V.tensor_copy(msc[:, :], q_hi[:, TCH:2 * TCH])
                V.tensor_tensor(msc[:, :], qf_h[:, TCH:2 * TCH], msc[:, :],
                                op=ALU.subtract)
                V.tensor_copy(q_lo[:, TCH:2 * TCH], msc[:, :])
                qdv_done[h] = cx.cnt["DV"]
                cx.wait("tensor", hsem)
                if HARDEN & 2:
                    fence()
                qth = q_hi

                cx.wait("tensor", "AC")
                cx.wait("tensor", "DV")
                Mcols = mpart[:, 8:16]
                mp_evt = {}
                for qt in range(8):
                    lq = qth[:, qt * 128:(qt + 1) * 128]
                    for ch in range(8):
                        si = (qt * 8 + ch) % 4
                        ps = p_sc2[:, si * TCH:(si + 1) * TCH]
                        if qt * 8 + ch >= 4:
                            cx.wait("tensor", "DV",
                                    mp_evt[qt * 8 + ch - 4])
                        nc.tensor.matmul(ps, lq,
                                         kt_h[:, ch * TCH:(ch + 1) * TCH],
                                         start=True, stop=True)
                        pe_tick()
                        cx.wait("vector", "PE")
                        V.reduce_max(mpart[:, ch:ch + 1], ps,
                                     axis=mybir.AxisListType.X)
                        mp_evt[qt * 8 + ch] = cx.cnt["DV"]
                    bs = p_ot[:, (qt % 2) * TCH:(qt % 2 + 1) * TCH]
                    if qt >= 2:
                        cx.wait("tensor", "DV", mp_evt[200 + qt - 2])
                    nc.tensor.matmul(
                        bs, lq, kbnd[:, (qt // 4) * TCH:(qt // 4 + 1) * TCH],
                        start=True, stop=True)
                    pe_tick()
                    cx.wait("vector", "PE")
                    V.tensor_tensor(msc[:, :], bs,
                                    bmk.ap()[:, qt * TCH:(qt + 1) * TCH],
                                    op=ALU.mult)
                    V.tensor_tensor(msc[:, :], msc[:, :],
                                    pnb.ap()[:, qt * TCH:(qt + 1) * TCH],
                                    op=ALU.add)
                    mp_evt[200 + qt] = cx.cnt["DV"]
                    V.reduce_max(Mcols[:, qt:qt + 1], msc[:, :],
                                 axis=mybir.AxisListType.X)
                    V.tensor_tensor(mpart[:, 0:8], mpart[:, 0:8],
                                    pen2.ap()[:, qt * 8:(qt + 1) * 8],
                                    op=ALU.add)
                    V.reduce_max(mpart[:, 0:1], mpart[:, 0:8],
                                 axis=mybir.AxisListType.X)
                    V.tensor_tensor(Mcols[:, qt:qt + 1], Mcols[:, qt:qt + 1],
                                    mpart[:, 0:1], op=ALU.max)
                cx.wait("tensor", "DV")
                nc.tensor.transpose(p_sc2[0:8, 0:128], Mcols, sb_id[:, :])
                pe_tick()
                cx.wait("vector", "PE")
                cx.wait("vector", "CWR")
                V.tensor_scalar(m8f.ap()[:, 0:128], p_sc2[0:8, 0:128], -1.0,
                                -1.0, op0=ALU.mult, op1=ALU.add)
                V.tensor_copy(mrow8.ap()[:, 0:128], m8f.ap()[:, 0:128])
                V.tensor_copy(m8f.ap()[:, 128:256], mrow8.ap()[:, 0:128])
                V.tensor_tensor(m8f.ap()[:, 128:256], m8f.ap()[:, 0:128],
                                m8f.ap()[:, 128:256], op=ALU.subtract)
                V.tensor_copy(mrow8.ap()[:, 128:256], m8f.ap()[:, 128:256])
                if h >= 1:
                    cx.wait("sync", "PE", kh_done[h - 1])
                cx.wait("sync", "DV")
                for j in range(8):
                    cx.dma("sync", "CWR", nrow.ap()[0:1, j * 128:(j + 1) * 128],
                           mrow8.ap()[j:j + 1, 0:128])
                    cx.dma("sync", "CWR",
                           nrow.ap()[0:1, F + j * 128: F + (j + 1) * 128],
                           mrow8.ap()[j:j + 1, 128:256])
                cx.wait("tensor", "CWR")
                if HARDEN & 4:
                    fence()

                for kt in range(32):
                    u = h * 32 + kt
                    sl2 = u % 2
                    ps = p_sc2[:, sl2 * F:(sl2 + 1) * F]
                    pt = ptb_t[:, sl2 * F:(sl2 + 1) * F]
                    pm = ptm_t[:, sl2 * F:(sl2 + 1) * F]
                    if u >= 2:
                        cx.wait("tensor", "AC", ac_evt[u - 2])
                    for qh in range(2):
                        psl = ps[:, qh * TCH:(qh + 1) * TCH]
                        nc.tensor.matmul(
                            psl, sb_ones_row[0:1, 0:128],
                            nrow.ap()[0:1, qh * TCH:(qh + 1) * TCH],
                            start=True, stop=False)
                        nc.tensor.matmul(
                            psl, sb_ones_row[0:1, 0:128],
                            nrow.ap()[0:1, F + qh * TCH: F + (qh + 1) * TCH],
                            start=False, stop=False)
                        nc.tensor.matmul(
                            psl, kt_h[:, kt * 128:(kt + 1) * 128],
                            q_lo[:, qh * TCH:(qh + 1) * TCH],
                            start=False, stop=False)
                        nc.tensor.matmul(
                            psl, kl_h[:, kt * 128:(kt + 1) * 128],
                            q_hi[:, qh * TCH:(qh + 1) * TCH],
                            start=False, stop=False)
                        nc.tensor.matmul(
                            psl, kt_h[:, kt * 128:(kt + 1) * 128],
                            q_hi[:, qh * TCH:(qh + 1) * TCH],
                            start=False, stop=True)
                    pe_tick()
                    cx.wait("scalar", "PE")
                    if u >= 2:
                        cx.wait("scalar", "DV", dv_evt[u - 2])
                    A.activation(pt[:, 0:TCH], ps[:, 0:TCH], AF.Exp,
                                 bias=0.0, scale=1.0)
                    A.activation(pt[:, TCH:2 * TCH], ps[:, TCH:2 * TCH],
                                 AF.Exp, bias=0.0, scale=1.0)
                    ac_evt[u] = cx.cnt["AC"]
                    cx.wait("vector", "AC")
                    if u >= 2:
                        cx.wait("vector", "PE", pe_evt[u - 2])
                    V.scalar_tensor_tensor(pm, qps[:, :], float(128 * kt), pt,
                                           op0=ALU.is_ge, op1=ALU.min)
                    dv_evt[u] = cx.cnt["DV"]
                    if kt == 0:
                        cx.wait("tensor", "DV")
                    else:
                        cx.wait("tensor", "DV", dv_evt[u])
                    for qh in range(2):
                        nc.tensor.matmul(
                            p_l[0:1, qh * TCH:(qh + 1) * TCH],
                            sb_ones_col[:, 0:1],
                            pm[:, qh * TCH:(qh + 1) * TCH],
                            start=(kt == 0), stop=(kt == 31))
                        nc.tensor.matmul(
                            p_ot[:, qh * TCH:(qh + 1) * TCH],
                            v_h[:, kt * 128:(kt + 1) * 128],
                            pm[:, qh * TCH:(qh + 1) * TCH],
                            start=(kt == 0), stop=(kt == 31))
                    pe_evt[u] = pe_tick()
                kh_done[h] = pe_evt[h * 32 + 31]
                if HARDEN & 8:
                    fence()
                cx.wait("vector", "PE")
                V.reciprocal(lrow.ap()[0:1, 0:F], p_l[0:1, 0:F])
                V.tensor_copy(rrow.ap()[0:1, 0:F], lrow.ap()[0:1, 0:F])
                V.tensor_copy(lrow.ap()[0:1, F:2 * F], rrow.ap()[0:1, 0:F])
                V.tensor_tensor(lrow.ap()[0:1, F:2 * F], lrow.ap()[0:1, 0:F],
                                lrow.ap()[0:1, F:2 * F], op=ALU.subtract)
                V.tensor_copy(rrow.ap()[0:1, F:2 * F], lrow.ap()[0:1, F:2 * F])
                cx.wait("tensor", "DV")
                cx.wait("tensor", "AC")
                for qh in range(2):
                    nc.tensor.matmul(
                        p_sc2[:, qh * TCH:(qh + 1) * TCH],
                        sb_ones_row[0:1, 0:128],
                        rrow.ap()[0:1, qh * TCH:(qh + 1) * TCH],
                        start=True, stop=False)
                    nc.tensor.matmul(
                        p_sc2[:, qh * TCH:(qh + 1) * TCH],
                        sb_ones_row[0:1, 0:128],
                        rrow.ap()[0:1, F + qh * TCH: F + (qh + 1) * TCH],
                        start=False, stop=True)
                pe_tick()
                cx.wait("scalar", "PE")
                cx.wait("scalar", "DV")
                A.copy(rbf[:, 0:F], p_sc2[:, 0:F])
                ac_evt[h * 32 + 30] = cx.cnt["AC"]
                cx.wait("vector", "AC")
                V.tensor_tensor(otf_t[:, h * F:(h + 1) * F], p_ot[:, 0:F],
                                rbf[:, 0:F], op=ALU.mult)
            fence()

        dev_tap("bo", lambda kc: otf_t[:, kc * F: kc * F + F], conv="bf16")

        with nc.sbuf_tensor(nm("rmw3"), [128, 2 * TCH], F32) as rmw3:
            def evac_outb(blk, tsl, p):
                gi = blk * (F // TCH) + tsl
                buf = gi % 2
                stn = "ST0" if buf == 0 else "ST1"
                xa = rmw3[:, buf * TCH:(buf + 1) * TCH]
                cx.wait("sync", stn)
                cx.dma("sync", "LD", xa,
                       x_str[blk * 128:(blk + 1) * 128,
                             tsl * TCH:(tsl + 1) * TCH])
                cx.wait("vector", "LD")
                cx.wait("vector", "PE")
                V.tensor_tensor(xa, xa, p[:, :], op=ALU.add)
                cx.wait("sync", "DV")
                cx.dma("sync", stn,
                       x_str[blk * 128:(blk + 1) * 128,
                             tsl * TCH:(tsl + 1) * TCH], xa)

            emit_fm_proj("w_outb", KC, otf, F, 3 * D, evac_outb)
            fence()

    if N_PHASES <= 3:
        dump_and_stop()

    emit_ffn(F, tap_prefix="f2")

    fence()
    for kc in range(KC):
        cx.dma("sync", "SQ", out[kc * 128:(kc + 1) * 128, :],
               x_str[kc * 128:(kc + 1) * 128, 0:F])
    cx.wait("sync", "SQ")
    cx.wait("gpsimd", "SQ")
    raise _StopEmission(nc, cx)


_CACHE = {}


def kernel(**inputs):
    key = "nc"
    if key not in _CACHE:
        _CACHE[key] = build_program()
    nc = _CACHE[key]
    in_maps = _host_inputs(inputs)
    res = run_bass_kernel_spmd(nc, in_maps, core_ids=list(range(8)))
    return _host_output(res.results)
